# version 07 (final)
# baseline (speedup 1.0000x reference)
import contextlib
from types import SimpleNamespace

import numpy as np

import concourse.bass as bass
import concourse.mybir as mybir
from concourse.bass_utils import run_bass_kernel_spmd

F32 = mybir.dt.float32
BF16 = mybir.dt.bfloat16
AF = mybir.ActivationFunctionType
ALU = mybir.AluOpType
AX = mybir.AxisListType

ENGINES = ("sy", "gp", "ve", "se", "te")


def _get_yscanmul():
    from concourse import dve_ops as _dve_ops
    for op in _dve_ops.OPS:
        if op.name == "ANT_YSCANMUL":
            return op
    from concourse.dve_spec import Spec, Src0, Src1, AluOp, scan as dscan, \
        lower
    from concourse.dve_op import DveOp
    spec = Spec(body=dscan(AluOp.ADD, Src0 * Src1))
    shas = {}
    for ver in ("v3", "v4"):
        try:
            shas[ver] = lower(spec, ver=ver, name="ANT_YSCANMUL").sha(ver)
        except Exception:
            pass
    op = DveOp("ANT_YSCANMUL", spec, subdim=False, uops_sha=shas,
               reference=lambda in0, in1, s0, s1, imm2:
               np.cumsum(in0 * in1, axis=-1))
    _dve_ops.OPS.append(op)
    _dve_ops._SUB_OPCODE_FOR_NAME[op.name] = (
        _dve_ops._CUSTOM_DVE_ROW_BASE + len(_dve_ops.OPS) - 1)
    assert _dve_ops._SUB_OPCODE_FOR_NAME[op.name] < 0x20
    _dve_ops.CUSTOM_DVE_SPECS[op.name] = op.spec
    _dve_ops.CUSTOM_DVE_REFS[op.name] = op.ref
    return op


def make_cfg(full=True):
    if full:
        c = SimpleNamespace(H=2048, E=4096, R=128, N=16, KCONV=4, FF=1024,
                            NE=16, NSH=2, TCORE=4096, TILE=512, B=2, NCORES=8)
    else:
        c = SimpleNamespace(H=256, E=512, R=128, N=16, KCONV=4, FF=256,
                            NE=16, NSH=2, TCORE=2048, TILE=512, B=2, NCORES=8)
    c.HC = c.H // 128
    c.EC = c.E // 128
    c.FC = c.FF // 128
    c.NT = c.TCORE // c.TILE
    c.NTIN = c.NT + 1
    c.TIN = c.NTIN * c.TILE
    c.SUBT = c.TILE // 128
    c.ECN = c.EC * c.N
    c.SEQC = c.NCORES // c.B
    c.L = c.SEQC * c.TCORE
    c.M2 = 2 * c.N
    c.RW = c.R + c.M2
    c.MRES = c.HC // 2
    c.NFQ = 4
    c.THALF = c.TCORE // c.NFQ
    c.NTH = c.THALF // c.TILE
    c.NSUBH = c.THALF // 128
    return c


class Sem:
    __slots__ = ("h", "count", "eng", "name")

    def __init__(self, h, name):
        self.h = h
        self.count = 0
        self.eng = None
        self.name = name


class MK:
    def __init__(self, cfg):
        self.c = cfg
        self.nc = bass.Bass("TRN2", target_bir_lowering=False)
        self.stack = contextlib.ExitStack()
        self.plan = {e: [] for e in ENGINES}
        self.lastw = {}
        self.sems = []
        self.ph = self.sem("ph")
        self.ph_total = 0
        self.nphase = 0
        self.vc = self.sem("vc")
        self.sc = self.sem("sc")
        self.tc = self.sem("tc")
        self.gc = self.sem("gc")
        self.selfsem = {"ve": self.sem("vself"), "se": self.sem("sself")}

    def sem(self, name):
        s = Sem(self.stack.enter_context(self.nc.semaphore(name)), name)
        self.sems.append(s)
        return s

    def _ins(self, eng, fn):
        self.plan[eng].append({"k": "raw", "fn": fn})

    def _op(self, eng, build, reads, writes, sem):
        semval = None
        if sem is not None:
            assert sem.eng in (None, eng), f"sem {sem.name}: {sem.eng} vs {eng}"
            sem.eng = eng
            sem.count += 1
            semval = sem.count
        self.plan[eng].append({
            "k": "op", "build": build, "sem": sem, "semval": semval,
            "reads": _names(reads), "writes": _names(writes),
            "self": False, "selfval": None,
        })

    def wait(self, eng, s, val=None):
        if val is None:
            val = s.count
        assert val <= s.count, f"wait-ahead on {s.name}: {val} > {s.count}"
        if val <= 0 or self.lastw.get((eng, id(s)), -1) >= val:
            return
        self.lastw[(eng, id(s))] = val
        self.plan[eng].append({"k": "wait", "sem": s, "val": val})

    def dma(self, eng, out, in_, s, slow=False):
        assert s.eng in (None, eng), f"sem {s.name} inc by {s.eng} and {eng}"
        s.eng = eng
        s.count += 16

        def f(e, out=out, in_=in_, h=s.h, slow=slow):
            if slow:
                e.dma_start(out=out, in_=in_,
                            allow_slow_non_contiguous=True).then_inc(h, 16)
            else:
                e.dma_start(out=out, in_=in_).then_inc(h, 16)
        self._ins(eng, f)

    def nopinc(self, eng, s):
        assert s.eng in (None, eng)
        s.eng = eng
        s.count += 1

        def f(e, h=s.h):
            if hasattr(e, "engine_nop"):
                e.engine_nop().then_inc(h, 1)
            else:
                e.nop().then_inc(h, 1)
        self._ins(eng, f)

    def barrier(self):
        self.nphase += 1
        target = 5 * self.nphase
        for eng in ENGINES:
            for s in self.sems:
                if s.eng == eng and s.count > 0 and s is not self.ph:
                    self.wait(eng, s, s.count)
            self.ph.count += 1

            def f(e, h=self.ph.h):
                if hasattr(e, "engine_nop"):
                    e.engine_nop().then_inc(h, 1)
                else:
                    e.nop().then_inc(h, 1)
            self._ins(eng, f)
        for eng in ENGINES:
            self.wait(eng, self.ph, target)

    def finalize_and_emit(self):
        for eng in ("ve", "se"):
            recs = self.plan[eng]
            written = {}
            readers = {}
            edges = {}
            for i, r in enumerate(recs):
                if r["k"] != "op":
                    continue
                for n in (r["reads"] | r["writes"]):
                    w = written.get(n)
                    if w is not None and w != i:
                        edges.setdefault(i, set()).add(w)
                for n in r["writes"]:
                    for rd in readers.get(n, ()):
                        if rd != i:
                            edges.setdefault(i, set()).add(rd)
                    readers[n] = []
                    written[n] = i
                for n in r["reads"]:
                    readers.setdefault(n, []).append(i)
            for ws in edges.values():
                for w in ws:
                    if recs[w]["sem"] is None:
                        recs[w]["self"] = True
            cnt = 0
            for r in recs:
                if r["k"] == "op" and r["self"]:
                    cnt += 1
                    r["selfval"] = cnt
            self.selfsem[eng].count = cnt
            self.selfsem[eng].eng = eng
            out = []
            high = {}
            for i, r in enumerate(recs):
                if r["k"] == "wait":
                    high[id(r["sem"])] = max(high.get(id(r["sem"]), 0),
                                             r["val"])
                if i in edges:
                    for w in sorted(edges[i]):
                        wr = recs[w]
                        if wr["sem"] is not None:
                            s, v = wr["sem"], wr["semval"]
                        else:
                            s, v = self.selfsem[eng], wr["selfval"]
                        if high.get(id(s), 0) < v:
                            high[id(s)] = v
                            out.append({"k": "wait", "sem": s, "val": v})
                out.append(r)
            self.plan[eng] = out

        def closures(eng):
            fns = []
            selfh = self.selfsem.get(eng)
            for r in self.plan[eng]:
                if r["k"] == "raw":
                    fns.append(r["fn"])
                elif r["k"] == "wait":
                    def f(e, h=r["sem"].h, v=r["val"]):
                        e.wait_ge(h, v)
                    fns.append(f)
                else:
                    def f(e, r=r, selfh=selfh):
                        inst = r["build"](e)
                        if r["sem"] is not None:
                            inst = inst.then_inc(r["sem"].h, 1)
                        if r["self"]:
                            inst.then_inc(selfh.h, 1)
                    fns.append(f)
            return fns

        with self.nc.Block() as block:
            block.sync(lambda e: [f(e) for f in closures("sy")])
            block.gpsimd(lambda e: [f(e) for f in closures("gp")])
            block.vector(lambda e: [f(e) for f in closures("ve")])
            block.scalar(lambda e: [f(e) for f in closures("se")])
            block.tensor(lambda e: [f(e) for f in closures("te")])

    def memset(self, ap, val=0.0, sem=None):
        def f(e, ap=ap, val=val):
            return e.memset(ap, val)
        self._op("ve", f, [], [ap], sem)

    def tt(self, out, a, b, op, sem=None, eng="ve"):
        def f(e, out=out, a=a, b=b, op=op):
            return e.tensor_tensor(out=out, in0=a, in1=b, op=op)
        self._op(eng, f, [a, b], [out], sem)

    def ts(self, out, a, s1, op, s2=None, op2=None, accum=None, sem=None,
           eng="ve"):
        def f(e, out=out, a=a, s1=s1, s2=s2, op=op, op2=op2, acc=accum):
            kw = {}
            if op2 is not None:
                kw["op1"] = op2
            if acc is not None:
                kw["accum_out"] = acc
            return e.tensor_scalar(out=out, in0=a, scalar1=s1, scalar2=s2,
                                   op0=op, **kw)
        self._op(eng, f, [a, s1, s2], [out, accum], sem)

    def stt(self, out, in0, scalar, in1, op0, op1, sem=None, eng="ve"):
        def f(e, out=out, in0=in0, s=scalar, in1=in1, op0=op0, op1=op1):
            return e.scalar_tensor_tensor(out=out, in0=in0, scalar=s, in1=in1,
                                          op0=op0, op1=op1)
        self._op(eng, f, [in0, scalar, in1], [out], sem)

    def copy(self, out, in_, sem=None, eng="ve"):
        def f(e, out=out, in_=in_):
            return e.tensor_copy(out=out, in_=in_)
        self._op(eng, f, [in_], [out], sem)

    def yscanmul(self, out, in0, in1, sem=None):
        op = _get_yscanmul()

        def f(e, out=out, in0=in0, in1=in1, op=op):
            return e._custom_dve(op, out=out, in0=in0, in1=in1)
        self._op("ve", f, [in0, in1], [out], sem)

    def vrecip(self, out, in_, sem=None):
        def f(e, out=out, in_=in_):
            return e.reciprocal(out=out, in_=in_)
        self._op("ve", f, [in_], [out], sem)

    def reduce(self, out, in_, op, sem=None, eng="ve"):
        def f(e, out=out, in_=in_, op=op):
            return e.tensor_reduce(out=out, in_=in_, op=op, axis=AX.X)
        self._op(eng, f, [in_], [out], sem)

    def scan(self, out, a, b, init, sem=None, op0=ALU.mult):
        def f(e, out=out, a=a, b=b, init=init, op0=op0):
            return e.tensor_tensor_scan(out=out, data0=a, data1=b,
                                        initial=init, op0=op0, op1=ALU.add)
        self._op("ve", f, [a, b, init], [out], sem)

    def act(self, out, in_, func, bias=0.0, scale=1.0, accum=None, sem=None):
        def f(e, out=out, in_=in_, func=func, bias=bias, scale=scale,
              acc=accum):
            kw = {}
            if acc is not None:
                kw["accum_out"] = acc
            return e.activation(out=out, in_=in_, func=func, bias=bias,
                                scale=scale, **kw)
        self._op("se", f, [in_, bias, scale], [out, accum], sem)

    def mm(self, out, lhsT, rhs, start, stop, sem=None):
        h = self._mark_te(sem)

        def f(e, out=out, lhsT=lhsT, rhs=rhs, start=start, stop=stop, h=h):
            i = e.matmul(out, lhsT, rhs, start=start, stop=stop)
            if h is not None:
                i.then_inc(h, 1)
        self._ins("te", f)

    def trans(self, out, in_, ident, sem=None):
        h = self._mark_te(sem)

        def f(e, out=out, in_=in_, ident=ident, h=h):
            i = e.transpose(out, in_, ident)
            if h is not None:
                i.then_inc(h, 1)
        self._ins("te", f)

    def _mark_te(self, s):
        if s is None:
            return None
        assert s.eng in (None, "te")
        s.eng = "te"
        s.count += 1
        return s.h

    def collective(self, kind, op, ins, outs, rg, sem):
        assert sem.eng in (None, "gp")
        sem.eng = "gp"
        sem.count += 1

        def f(e, kind=kind, op=op, ins=ins, outs=outs, rg=rg, h=sem.h):
            e.collective_compute(kind, op, ins=ins, outs=outs,
                                 replica_groups=rg).then_inc(h, 1)
        self._ins("gp", f)

    def rot(self, tiles):
        return Rot(self, tiles)

    def ring(self, name, tiles, qeng):
        return Ring(self, name, tiles, qeng)


def _names(items):
    out = set()
    for it in items:
        if it is None or isinstance(it, (int, float)):
            continue
        try:
            out.add(it.tensor.name)
        except AttributeError:
            pass
    return out


class Rot:

    def __init__(self, mk, tiles):
        self.mk = mk
        self.tiles = tiles
        self.rel = [[] for _ in tiles]
        self.idx = 0

    def acquire(self, eng):
        i = self.idx % len(self.tiles)
        for (s, v) in self.rel[i]:
            self.mk.wait(eng, s, v)
        self.rel[i] = []
        self.idx += 1
        return SimpleNamespace(t=self.tiles[i], i=i)

    def record(self, slot, s):
        self.rel[slot.i].append((s, s.count))


class Ring:

    def __init__(self, mk, name, tiles, qeng):
        self.mk = mk
        self.tiles = tiles
        self.qeng = qeng
        self.rsem = [mk.sem(f"{name}{i}") for i in range(len(tiles))]
        self.rel = [[] for _ in tiles]
        self.idx = 0

    def load(self, src, slow=False):
        i = self.idx % len(self.tiles)
        for (s, v) in self.rel[i]:
            self.mk.wait(self.qeng, s, v)
        self.rel[i] = []
        self.mk.dma(self.qeng, self.tiles[i][:], src, self.rsem[i], slow)
        self.idx += 1
        return SimpleNamespace(t=self.tiles[i], i=i,
                               ready=(self.rsem[i], self.rsem[i].count))

    def cwait(self, eng, slot):
        self.mk.wait(eng, slot.ready[0], slot.ready[1])

    def record(self, slot, s):
        self.rel[slot.i].append((s, s.count))


def build_program(c, dbg=False):
    mk = MK(c)
    nc = mk.nc
    st = mk.stack
    T = c.TILE

    def din(name, shape, dt=F32):
        return nc.dram_tensor(name, list(shape), dt, kind="ExternalInput").ap()

    x_in = din("x", (c.TIN, c.H))
    w_in_T = din("w_in_T", (c.H, 2 * c.E))
    w_xp_T = din("w_xp_T", (c.E, c.RW))
    w_dt_T = din("w_dt_T", (c.R, c.E))
    w_out_T = din("w_out_T", (c.E, c.H))
    conv_w_l = din("conv_w_l", (128, c.EC * c.KCONV))
    conv_b_l = din("conv_b_l", (128, c.EC))
    dt_b_l = din("dt_b_l", (128, c.EC))
    D_l = din("D_l", (128, c.EC))
    A_log_l = din("A_log_l", (128, c.ECN))
    mask8_l = din("mask8_l", (128, c.NCORES))
    ident_in = din("ident", (128, 128))
    sel32_in = din("sel32", (c.M2, c.M2 * 128))
    sel16_in = din("sel16", (c.NE, c.NE * 128))
    wa_in = din("wa", (c.H, c.NE))
    rg_in = din("rg", (c.NE, c.H, c.FF))
    rw1_in = din("rw1", (c.NE, c.H, c.FF))
    rw2_in = din("rw2", (c.NE, c.FF, c.H))
    rgb_in = din("rgb_l", (128, c.NE * c.FC))
    rb1_in = din("rb1_l", (128, c.NE * c.FC))
    rb2_in = din("rb2", (c.NE, c.H))
    sg_in = din("sg", (c.NSH, c.H, c.FF))
    sw1_in = din("sw1", (c.NSH, c.H, c.FF))
    sw2_in = din("sw2", (c.NSH, c.FF, c.H))
    sgb_in = din("sgb_l", (128, c.NSH * c.FC))
    sb1_in = din("sb1_l", (128, c.NSH * c.FC))
    sb2_in = din("sb2_l", (128, c.NSH * c.HC))

    out_d = nc.dram_tensor("out", [c.TCORE, c.H], F32,
                           kind="ExternalOutput").ap()

    def scratch(name, shape, dt):
        kind = "ExternalOutput" if dbg else "Internal"
        return nc.dram_tensor(name, list(shape), dt, kind=kind).ap()

    sc_gate = scratch("sc_gate", (c.E, c.TCORE), BF16)
    sc_xconv = scratch("sc_xconv", (c.E, c.TCORE), BF16)
    sc_dt = scratch("sc_dt", (c.E, c.TCORE), F32)
    sc_y = scratch("sc_y", (c.E, c.TCORE), F32)
    sc_yt = scratch("sc_yt", (c.E, c.TCORE), BF16)
    sc_x2 = scratch("sc_x2", (c.TCORE, c.H), F32)
    sc_x2nT = scratch("sc_x2nT", (c.H, c.TCORE), BF16)
    dbg_xnT = None
    if dbg:
        dbg_xnT = nc.dram_tensor("dbg_xnT", [c.H, c.TIN], BF16,
                                 kind="ExternalOutput").ap()
        dbg_xnb = nc.dram_tensor("dbg_xnb", [128, c.H], BF16,
                                 kind="ExternalOutput").ap()
        dbg_stat = nc.dram_tensor("dbg_stat", [128, 8], F32,
                                 kind="ExternalOutput").ap()
        dbg_comb = nc.dram_tensor("dbg_comb", [16, 16384], BF16,
                                  kind="ExternalOutput").ap()
    cc_w = c.ECN + c.EC
    cc_in = nc.dram_tensor("cc_in", [128, cc_w], F32, kind="Internal").ap()
    cc_out = nc.dram_tensor("cc_out", [c.NCORES * 128, cc_w], F32,
                            kind="Internal").ap()

    def sb(name, shape, dt):
        return st.enter_context(nc.sbuf_tensor(name, list(shape), dt))

    def ps(name, shape, dt=F32):
        return st.enter_context(nc.psum_tensor(name, list(shape), dt))

    s_ld = mk.sem("ld")
    s_gld = mk.sem("gld")
    s_st = mk.sem("st")
    s_out = mk.sem("sout")

    ident_f = sb("ident_f", (128, 128), F32)
    ident_b = sb("ident_b", (128, 128), BF16)
    convw_t = sb("convw_t", (128, c.EC * c.KCONV), F32)
    convb_t = sb("convb_t", (128, c.EC), F32)
    dtb_t = sb("dtb_t", (128, c.EC), F32)
    Dl_t = sb("Dl_t", (128, c.EC), F32)
    A_t = sb("A_t", (128, c.ECN), F32)
    mask_t = sb("mask_t", (128, c.NCORES), F32)
    state_t = sb("state_t", (128, c.ECN), F32)
    dtsum_t = sb("dtsum_t", (128, c.EC), F32)
    cumst_t = sb("cumst_t", (128, c.EC), F32)
    carry_t = sb("carry_t", (128, c.ECN), F32)

    cst_vals = [1e-5, -1.0, -0.5, 0.5]
    cst_t = sb("cstt", (128, len(cst_vals)), F32)
    for i, v in enumerate(cst_vals):
        mk.memset(cst_t[:, i:i + 1], v)
        nc.const_aps.register(F32, v, cst_t[:, i:i + 1])

    mk.dma("sy", ident_f[:], ident_in, s_ld)
    mk.dma("gp", ident_b[:], ident_in, s_gld)
    mk.dma("sy", convw_t[:], conv_w_l, s_ld)
    mk.dma("sy", convb_t[:], conv_b_l, s_ld)
    mk.dma("sy", dtb_t[:], dt_b_l, s_ld)
    mk.dma("sy", Dl_t[:], D_l, s_ld)
    mk.dma("sy", A_t[:], A_log_l, s_ld)
    mk.dma("sy", mask_t[:], mask8_l, s_ld)
    mk.wait("se", s_ld)
    mk.act(A_t[:], A_t[:], AF.Exp, sem=mk.sc)
    mk.wait("ve", mk.sc)
    mk.ts(A_t[:], A_t[:], -1.0, ALU.mult, sem=mk.vc)
    mk.memset(state_t[:], 0.0)
    mk.memset(dtsum_t[:], 0.0)
    mk.memset(cumst_t[:], 0.0)
    mk.memset(carry_t[:], 0.0, sem=mk.vc)
    mk.barrier()

    with contextlib.ExitStack() as ph:
        def psb(name, shape, dt):
            return ph.enter_context(nc.sbuf_tensor(name, list(shape), dt))

        def psp(name, shape):
            return ph.enter_context(nc.psum_tensor(name, list(shape), F32))

        xnT = [psb(f"xnT{j}", (128, c.TIN), BF16) for j in range(c.HC)]
        xring = mk.ring("xr", [psb(f"xr{i}", (128, c.H), F32)
                               for i in range(2)], "sy")
        dump = psb("dump", (128, c.H), F32)
        xnb = mk.rot([psb(f"xnb{i}", (128, c.H), BF16) for i in range(2)])
        stat = psb("stat", (128, 8), F32)
        tp = mk.rot([ph.enter_context(
            nc.psum_tensor(f"tp{i}", [128, 512], BF16)) for i in range(4)])

        for t in range(c.NTIN):
            for s_ in range(c.SUBT):
                row = t * T + s_ * 128
                xs = xring.load(x_in[row:row + 128, :])
                xring.cwait("ve", xs)
                m_ = stat[:, 0:1]
                v_ = stat[:, 1:2]
                r_ = stat[:, 2:3]
                q_ = stat[:, 3:4]
                mk.reduce(m_, xs.t[:], ALU.add)
                mk.ts(m_, m_, 1.0 / c.H, ALU.mult, sem=mk.vc)
                xring.cwait("se", xs)
                mk.act(dump[:], xs.t[:], AF.Square, accum=q_, sem=mk.sc)
                mk.wait("ve", mk.sc)
                mk.ts(v_, q_, 1.0 / c.H, ALU.mult)
                mk.ts(r_, m_, m_, ALU.mult)
                mk.tt(v_, v_, r_, ALU.subtract)
                mk.ts(v_, v_, 0.0, ALU.max, sem=mk.vc)
                mk.wait("se", mk.vc)
                mk.act(r_, v_, AF.Ln, bias=1e-5)
                mk.act(r_, r_, AF.Exp, scale=-0.5, sem=mk.sc)
                xslot = xnb.acquire("ve")
                mk.wait("ve", mk.sc)
                mk.ts(xslot.t[:], xs.t[:], m_, ALU.subtract, s2=r_,
                      op2=ALU.mult, sem=mk.vc)
                xring.record(xs, mk.vc)
                if dbg and t == 1 and s_ == 0:
                    mk.wait("sy", mk.vc)
                    mk.dma("sy", dbg_xnb, xslot.t[:], s_st)
                    mk.dma("sy", dbg_stat, stat[:], s_st)
                for g0 in range(0, c.HC, 4):
                    gn = min(4, c.HC - g0)
                    pslot = tp.acquire("te")
                    mk.wait("te", mk.vc)
                    for jj in range(gn):
                        j = g0 + jj
                        mk.trans(pslot.t[:, jj * 128:(jj + 1) * 128],
                                 xslot.t[:, j * 128:(j + 1) * 128], ident_b[:],
                                 sem=(mk.tc if jj == gn - 1 else None))
                    mk.wait("se", mk.tc)
                    for jj in range(gn):
                        j = g0 + jj
                        mk.act(xnT[j][:, row:row + 128],
                               pslot.t[:, jj * 128:(jj + 1) * 128], AF.Copy,
                               sem=(mk.sc if jj == gn - 1 else None))
                    tp.record(pslot, mk.sc)
                xnb.record(xslot, mk.tc)
        mk.barrier()

        with contextlib.ExitStack() as p1b:
            def bsb(name, shape, dt):
                return p1b.enter_context(nc.sbuf_tensor(name, list(shape), dt))

            wring = mk.ring("wi", [bsb(f"wi{i}", (128, c.HC * 128), BF16)
                                   for i in range(2)], "gp")
            cbufs = [bsb(f"cbuf{i}", (128, T + 3), BF16) for i in range(2)]
            accb = mk.rot([bsb(f"acc{i}", (128, T), F32) for i in range(2)])
            xcvb = [bsb(f"xcv{i}", (128, T), BF16) for i in range(2)]
            xcv_ss = [mk.sem(f"xcvs{i}") for i in range(2)]
            xcv_rel = [[] for _ in range(2)]
            gpsum = mk.rot([ph.enter_context(
                nc.psum_tensor(f"gps{i}", [128, T], F32)) for i in range(3)])
            gbuf = [bsb(f"gbf{i}", (128, T), BF16) for i in range(2)]
            gb_ss = [mk.sem(f"gbs{i}") for i in range(2)]
            gb_rel = [[] for _ in range(2)]
            w_src = w_in_T.rearrange("(k p) m -> p k m", p=128)

            for m in range(2 * c.EC):
                ws = wring.load(w_src[:, :, m * 128:(m + 1) * 128])
                is_u = m < c.EC
                for t in range(0 if is_u else 1, c.NTIN):
                    pslot = gpsum.acquire("te")
                    wring.cwait("te", ws)
                    for k in range(c.HC):
                        mk.mm(pslot.t[:], ws.t[:, k * 128:(k + 1) * 128],
                              xnT[k][:, t * T:(t + 1) * T],
                              start=(k == 0), stop=(k == c.HC - 1),
                              sem=(mk.tc if k == c.HC - 1 else None))
                    if t == c.NTIN - 1:
                        wring.record(ws, mk.tc)
                    mk.wait("se", mk.tc)
                    if is_u:
                        ec = m
                        cb = cbufs[m % 2][:]
                        mk.wait("se", mk.vc)
                        mk.act(cb[:, 3:T + 3], pslot.t[:], AF.Copy, sem=mk.sc)
                        gpsum.record(pslot, mk.sc)
                        mk.wait("ve", mk.sc)
                        if t > 0:
                            aslot = accb.acquire("ve")
                            wl = convw_t[:, ec * c.KCONV:ec * c.KCONV + 1]
                            mk.ts(aslot.t[:], cb[:, 0:T], wl, ALU.mult)
                            for kk in range(1, c.KCONV):
                                wl = convw_t[:, ec * c.KCONV + kk:
                                             ec * c.KCONV + kk + 1]
                                mk.stt(aslot.t[:], cb[:, kk:kk + T], wl,
                                       aslot.t[:], ALU.mult, ALU.add)
                        mk.copy(cb[:, 0:3], cb[:, T:T + 3], sem=mk.vc)
                        if t > 0:
                            xi = (t - 1) % 2
                            for (s2, v2) in xcv_rel[xi]:
                                mk.wait("se", s2, v2)
                            xcv_rel[xi] = []
                            mk.wait("se", mk.vc)
                            mk.act(xcvb[xi][:], aslot.t[:], AF.Silu,
                                   bias=convb_t[:, ec:ec + 1], sem=mk.sc)
                            accb.record(aslot, mk.sc)
                            mk.wait("sy", mk.sc)
                            mk.dma("sy", sc_xconv[ec * 128:(ec + 1) * 128,
                                                  (t - 1) * T:t * T],
                                   xcvb[xi][:], xcv_ss[xi])
                            xcv_rel[xi].append((xcv_ss[xi], xcv_ss[xi].count))
                    else:
                        ec = m - c.EC
                        gi = (t - 1) % 2
                        for (s2, v2) in gb_rel[gi]:
                            mk.wait("se", s2, v2)
                        gb_rel[gi] = []
                        mk.act(gbuf[gi][:], pslot.t[:], AF.Copy, sem=mk.sc)
                        gpsum.record(pslot, mk.sc)
                        mk.wait("sy", mk.sc)
                        mk.dma("sy", sc_gate[ec * 128:(ec + 1) * 128,
                                             (t - 1) * T:t * T],
                               gbuf[gi][:], gb_ss[gi])
                        gb_rel[gi].append((gb_ss[gi], gb_ss[gi].count))
            mk.barrier()

    with contextlib.ExitStack() as p2s:
        def bsb(name, shape, dt):
            return p2s.enter_context(nc.sbuf_tensor(name, list(shape), dt))

        def bps(name, shape):
            return p2s.enter_context(nc.psum_tensor(name, list(shape), F32))

        dtlow_t = bsb("dtlow", (128, c.TCORE), BF16)
        BC_t = bsb("BC", (c.M2, c.TCORE), BF16)
        wxp_t = bsb("wxp", (128, c.EC * c.RW), BF16)
        wdt_t = bsb("wdt", (128, c.E), BF16)
        sel32_t = bsb("sel32_s", (c.M2, c.M2 * 128), BF16)

        mk.dma("gp", wxp_t[:], w_xp_T.rearrange("(k p) r -> p k r", p=128),
               s_gld)
        mk.dma("gp", wdt_t[:], w_dt_T, s_gld)
        mk.dma("gp", sel32_t[:], sel32_in, s_gld)

        with contextlib.ExitStack() as p1c:
            def csb(name, shape, dt):
                return p1c.enter_context(nc.sbuf_tensor(name, list(shape), dt))
            xcring = mk.ring("xdc", [csb(f"xdc{i}", (128, T), BF16)
                                     for i in range(3)], "sy")
            psA = mk.rot([p1c.enter_context(
                nc.psum_tensor(f"pA{i}", [128, T], F32)) for i in range(2)])
            psB = mk.rot([p1c.enter_context(
                nc.psum_tensor(f"pB{i}", [c.M2, T], F32)) for i in range(2)])
            mk.wait("te", s_gld)
            for tt in range(c.NT):
                sA = psA.acquire("te")
                sB = psB.acquire("te")
                for k in range(c.EC):
                    xs = xcring.load(sc_xconv[k * 128:(k + 1) * 128,
                                              tt * T:(tt + 1) * T])
                    xcring.cwait("te", xs)
                    mk.mm(sA.t[:], wxp_t[:, k * c.RW:k * c.RW + c.R],
                          xs.t[:], start=(k == 0), stop=(k == c.EC - 1),
                          sem=(mk.tc if k == c.EC - 1 else None))
                    mk.mm(sB.t[:], wxp_t[:, k * c.RW + c.R:(k + 1) * c.RW],
                          xs.t[:], start=(k == 0), stop=(k == c.EC - 1),
                          sem=mk.tc)
                    xcring.record(xs, mk.tc)
                mk.wait("se", mk.tc)
                mk.act(dtlow_t[:, tt * T:(tt + 1) * T], sA.t[:], AF.Copy,
                       sem=mk.sc)
                psA.record(sA, mk.sc)
                mk.wait("ve", mk.tc)
                mk.copy(BC_t[:, tt * T:(tt + 1) * T], sB.t[:], sem=mk.vc)
                psB.record(sB, mk.vc)
            mk.barrier()

        with contextlib.ExitStack() as p2:
            def csb(name, shape, dt):
                return p2.enter_context(nc.sbuf_tensor(name, list(shape), dt))
            Brep = csb("Brep", (128, c.N * T), F32)
            Crep = csb("Crep", (128, c.N * T), F32)
            hbuf = csb("hbuf", (128, c.N * T), F32)
            Rbuf = csb("Rbuf", (128, c.N * T), F32)
            dApp = [csb(f"dApp{i}", (128, T), F32) for i in range(2)]
            dtt = [csb(f"dtt{i}", (128, T), F32) for i in range(2)]
            dt_ss = [mk.sem(f"dts{i}") for i in range(2)]
            dt_rel = [[] for _ in range(2)]
            dtu_t = csb("dtu", (128, T), F32)
            wb3 = [csb(f"wbt{i}", (128, T), F32) for i in range(3)]
            ya = [csb(f"ya{i}", (128, T), F32) for i in range(2)]
            ya_ss = [mk.sem(f"yas{i}") for i in range(2)]
            ya_rel = [[] for _ in range(2)]
            tA = csb("tA", (128, T), F32)
            tB = csb("tB", (128, T), F32)
            colt = csb("colt", (128, 4), F32)
            xc2 = mk.ring("xc2", [csb(f"xc2{i}", (128, T), BF16)
                                  for i in range(3)], "sy")
            pdt = mk.rot([p2.enter_context(
                nc.psum_tensor(f"pdt{i}", [128, T], F32)) for i in range(2)])
            psel = mk.rot([p2.enter_context(
                nc.psum_tensor(f"psel{i}", [128, T], F32)) for i in range(2)])
            hb3 = hbuf.rearrange("p (t n) -> p t n", n=c.N)
            cr3 = Crep.rearrange("p (t n) -> p t n", n=c.N)
            rb3 = Rbuf.rearrange("p (t n) -> p t n", n=c.N)

            for tb in range(c.NT):
                tsl = slice(tb * T, (tb + 1) * T)
                for n in range(2 * c.N):
                    sl = psel.acquire("te")
                    if tb > 0 and n < 2:
                        mk.wait("te", mk.vc)
                    mk.mm(sl.t[:], sel32_t[:, n * 128:(n + 1) * 128],
                          BC_t[:, tsl], start=True, stop=True, sem=mk.tc)
                    mk.wait("se", mk.tc)
                    nn = n % c.N
                    if n < c.N:
                        mk.act(Brep[:, nn * T:(nn + 1) * T], sl.t[:], AF.Copy,
                               sem=mk.sc)
                    else:
                        mk.act(cr3[:, :, nn], sl.t[:], AF.Copy, sem=mk.sc)
                    psel.record(sl, mk.sc)
                mk.wait("ve", mk.sc)
                mk.wait("te", mk.sc)
                for ec in range(c.EC):
                    pd = pdt.acquire("te")
                    mk.mm(pd.t[:], wdt_t[:, ec * 128:(ec + 1) * 128],
                          dtlow_t[:, tsl], start=True, stop=True, sem=mk.tc)
                    mk.wait("se", mk.tc)
                    bcol = dtb_t[:, ec:ec + 1]
                    mk.act(tA[:], pd.t[:], AF.Abs, bias=bcol)
                    mk.act(tA[:], tA[:], AF.Exp, scale=-1.0)
                    mk.act(tA[:], tA[:], AF.Ln, bias=1.0,
                           accum=colt[:, 0:1])
                    mk.act(tB[:], pd.t[:], AF.Relu, bias=bcol,
                           accum=colt[:, 1:2], sem=mk.sc)
                    pdt.record(pd, mk.sc)
                    di = (tb * c.EC + ec) % 2
                    for (s2, v2) in dt_rel[di]:
                        mk.wait("ve", s2, v2)
                    dt_rel[di] = []
                    mk.wait("ve", mk.sc)
                    mk.tt(dtt[di][:], tA[:], tB[:], ALU.add)
                    mk.tt(colt[:, 2:3], colt[:, 0:1], colt[:, 1:2], ALU.add)
                    mk.tt(dtsum_t[:, ec:ec + 1], dtsum_t[:, ec:ec + 1],
                          colt[:, 2:3], ALU.add, sem=mk.vc)
                    mk.wait("sy", mk.vc)
                    mk.dma("sy", sc_dt[ec * 128:(ec + 1) * 128, tsl],
                           dtt[di][:], dt_ss[di])
                    dt_rel[di].append((dt_ss[di], dt_ss[di].count))
                    xs = xc2.load(sc_xconv[ec * 128:(ec + 1) * 128, tsl])
                    xc2.cwait("ve", xs)
                    mk.tt(dtu_t[:], dtt[di][:], xs.t[:], ALU.mult, sem=mk.vc)
                    xc2.record(xs, mk.vc)
                    dt_rel[di].append((mk.vc, mk.vc.count))
                    mk.wait("se", mk.vc)
                    yi = (tb * c.EC + ec) % 2
                    for (s2, v2) in ya_rel[yi]:
                        mk.wait("ve", s2, v2)
                    ya_rel[yi] = []
                    scan_vc = []
                    for n in range(3):
                        mk.tt(wb3[n][:], dtu_t[:], Brep[:, n * T:(n + 1) * T],
                              ALU.mult)
                    for n in range(c.N):
                        if n >= 2:
                            mk.wait("se", mk.vc, scan_vc[n - 2])
                        mk.act(dApp[n % 2][:], dtt[di][:], AF.Exp,
                               scale=A_t[:, ec * c.N + n:ec * c.N + n + 1],
                               sem=mk.sc)
                        mk.wait("ve", mk.sc)
                        mk.scan(hb3[:, :, n], dApp[n % 2][:], wb3[n % 3][:],
                                state_t[:, ec * c.N + n:ec * c.N + n + 1],
                                sem=mk.vc)
                        scan_vc.append(mk.vc.count)
                        if n + 3 < c.N:
                            mk.tt(wb3[(n + 3) % 3][:], dtu_t[:],
                                  Brep[:, (n + 3) * T:(n + 4) * T], ALU.mult)
                    dt_rel[di].append((mk.sc, mk.sc.count))
                    mk.yscanmul(Rbuf[:], hbuf[:], Crep[:], sem=mk.vc)
                    mk.copy(ya[yi][:, 0:1], rb3[:, 0:1, c.N - 1])
                    mk.tt(ya[yi][:, 1:T], rb3[:, 1:T, c.N - 1],
                          rb3[:, 0:T - 1, c.N - 1], ALU.subtract)
                    mk.copy(state_t[:, ec * c.N:(ec + 1) * c.N],
                            hbuf[:, (T - 1) * c.N:T * c.N], sem=mk.vc)
                    mk.wait("sy", mk.vc)
                    mk.dma("sy", sc_y[ec * 128:(ec + 1) * 128, tsl],
                           ya[yi][:], ya_ss[yi])
                    ya_rel[yi].append((ya_ss[yi], ya_ss[yi].count))
            mk.barrier()

        ccall = bsb("ccall", (128, c.NCORES * cc_w), F32)
        Ptot_t = bsb("Ptot", (128, c.ECN), F32)
        ctmp = [bsb(f"ctmp{i}", (128, c.ECN), F32) for i in range(3)]
        mk.dma("sy", cc_in[:, 0:c.ECN], state_t[:], s_st)
        mk.dma("sy", cc_in[:, c.ECN:cc_w], dtsum_t[:], s_st)
        mk.wait("gp", s_st)
        mk.collective("AllGather", ALU.bypass, [cc_in], [cc_out],
                      [list(range(c.NCORES))], mk.gc)
        mk.wait("sy", mk.gc)
        for j in range(c.NCORES):
            mk.dma("sy", ccall[:, j * cc_w:(j + 1) * cc_w],
                   cc_out[j * 128:(j + 1) * 128, :], s_ld)
        mk.wait("se", s_ld)
        mk.wait("ve", s_ld)
        for j in range(c.NCORES):
            base = j * cc_w
            for ec in range(c.EC):
                mk.act(Ptot_t[:, ec * c.N:(ec + 1) * c.N],
                       A_t[:, ec * c.N:(ec + 1) * c.N], AF.Exp,
                       scale=ccall[:, base + c.ECN + ec:base + c.ECN + ec + 1],
                       sem=(mk.sc if ec == c.EC - 1 else None))
            mk.wait("ve", mk.sc)
            mk.tt(ctmp[0][:], Ptot_t[:], carry_t[:], ALU.mult)
            mk.tt(ctmp[1][:], ctmp[0][:], ccall[:, base:base + c.ECN],
                  ALU.add)
            mk.tt(ctmp[2][:], ctmp[1][:], carry_t[:], ALU.subtract)
            mk.stt(carry_t[:], ctmp[2][:], mask_t[:, j:j + 1], carry_t[:],
                   ALU.mult, ALU.add, sem=mk.vc)
            mk.wait("se", mk.vc)
        mk.barrier()

        with contextlib.ExitStack() as p3:
            def csb(name, shape, dt):
                return p3.enter_context(nc.sbuf_tensor(name, list(shape), dt))
            Crep = csb("Crep3", (128, c.N * T), F32)
            Pband = csb("Pband", (128, c.N * T), F32)
            zbuf = csb("zbuf", (128, T), F32)
            cum_t = csb("cum", (128, T), F32)
            gb_t = csb("gb3", (128, 2 * T), F32)
            yc_t = csb("yc3", (128, T), F32)
            ytb = [csb(f"ytb{i}", (128, T), BF16) for i in range(2)]
            yt_ss = [mk.sem(f"yts{i}") for i in range(2)]
            yt_rel = [[] for _ in range(2)]
            dring = mk.ring("dr3", [csb(f"dr3{i}", (128, T), F32)
                                    for i in range(3)], "sy")
            yring = mk.ring("yr3", [csb(f"yr3{i}", (128, T), F32)
                                    for i in range(3)], "sy")
            psel = mk.rot([p3.enter_context(
                nc.psum_tensor(f"ps3{i}", [128, T], F32)) for i in range(2)])
            mk.memset(zbuf[:], 0.0, sem=mk.vc)

            for tb in range(c.NT):
                tsl = slice(tb * T, (tb + 1) * T)
                for n in range(c.N):
                    sl = psel.acquire("te")
                    if tb > 0 and n < 2:
                        mk.wait("te", mk.vc)
                    mk.mm(sl.t[:], sel32_t[:, (c.N + n) * 128:
                                           (c.N + n + 1) * 128],
                          BC_t[:, tsl], start=True, stop=True, sem=mk.tc)
                    mk.wait("se", mk.tc)
                    mk.act(Crep[:, n * T:(n + 1) * T], sl.t[:], AF.Copy,
                           sem=mk.sc)
                    psel.record(sl, mk.sc)
                mk.wait("ve", mk.sc)
                for ec in range(c.EC):
                    ds = dring.load(sc_dt[ec * 128:(ec + 1) * 128, tsl])
                    ys = yring.load(sc_y[ec * 128:(ec + 1) * 128, tsl])
                    dring.cwait("ve", ds)
                    mk.scan(cum_t[:], ds.t[:], zbuf[:],
                            cumst_t[:, ec:ec + 1], sem=mk.vc, op0=ALU.add)
                    dring.record(ds, mk.vc)
                    mk.copy(cumst_t[:, ec:ec + 1], cum_t[:, T - 1:T],
                            sem=mk.vc)
                    mk.wait("se", mk.vc)
                    for n in range(c.N):
                        mk.act(Pband[:, n * T:(n + 1) * T], cum_t[:], AF.Exp,
                               scale=A_t[:, ec * c.N + n:ec * c.N + n + 1],
                               sem=mk.sc)
                    gb2 = gb_t.rearrange("p (i t) -> p i t", i=2)
                    psc0 = mk.sc.count - c.N
                    for n in range(c.N):
                        mk.wait("ve", mk.sc, psc0 + n + 1)
                        mk.tt(gb2[:, n % 2, :], Pband[:, n * T:(n + 1) * T],
                              Crep[:, n * T:(n + 1) * T], ALU.mult)
                        if n > 0:
                            col = carry_t[:, ec * c.N + n - 1:ec * c.N + n]
                            if n == 1:
                                mk.ts(yc_t[:], gb2[:, 0, :], col, ALU.mult)
                            else:
                                mk.stt(yc_t[:], gb2[:, (n - 1) % 2, :], col,
                                       yc_t[:], ALU.mult, ALU.add)
                    col = carry_t[:, ec * c.N + c.N - 1:ec * c.N + c.N]
                    mk.stt(yc_t[:], gb2[:, (c.N - 1) % 2, :], col, yc_t[:],
                           ALU.mult, ALU.add)
                    yi = (tb * c.EC + ec) % 2
                    for (s2, v2) in yt_rel[yi]:
                        mk.wait("ve", s2, v2)
                    yt_rel[yi] = []
                    yring.cwait("ve", ys)
                    mk.tt(ytb[yi][:], yc_t[:], ys.t[:], ALU.add, sem=mk.vc)
                    yring.record(ys, mk.vc)
                    mk.wait("sy", mk.vc)
                    mk.dma("sy", sc_yt[ec * 128:(ec + 1) * 128, tsl],
                           ytb[yi][:], yt_ss[yi])
                    yt_rel[yi].append((yt_ss[yi], yt_ss[yi].count))
            mk.barrier()

    with contextlib.ExitStack() as p4:
        def csb(name, shape, dt):
            return p4.enter_context(nc.sbuf_tensor(name, list(shape), dt))
        nA = c.MRES
        woA = csb("woA", (128, c.EC * nA * 128), BF16)
        wo_src = w_out_T.rearrange("(k p) m -> p k m", p=128)
        mk.dma("gp", woA[:], wo_src[:, :, 0:nA * 128], s_gld)
        mk.wait("te", s_gld)
        wob = mk.ring("wob", [csb(f"wob{i}", (128, c.EC * 128), BF16)
                              for i in range(2)], "gp")
        ygb = csb("ygb", (128, c.EC * T), BF16)
        sgb = mk.rot([csb(f"sgb{i}", (128, T), F32) for i in range(2)])
        ydb = csb("ydb", (128, T), F32)
        ostg = mk.rot([csb(f"ostg{i}", (128, T), F32) for i in range(2)])
        x2b = []
        x2_ss = []
        for i in range(c.SUBT):
            x2b.append(csb(f"x2b{i}", (128, c.H), F32))
            x2_ss.append(mk.sem(f"x2s{i}"))
        x2_rel = [[] for _ in range(c.SUBT)]
        x2_ld = [mk.sem(f"x2l{i}") for i in range(c.SUBT)]
        ytr = mk.ring("ytr", [csb(f"ytr{i}", (128, T), BF16)
                              for i in range(3)], "sy")
        gtr = mk.ring("gtr", [csb(f"gtr{i}", (128, T), BF16)
                              for i in range(3)], "sy")
        xcr = mk.ring("xcr", [csb(f"xcr{i}", (128, T), BF16)
                              for i in range(3)], "sy")
        pso = mk.rot([p4.enter_context(
            nc.psum_tensor(f"po4{i}", [128, T], F32)) for i in range(2)])
        psT = mk.rot([p4.enter_context(
            nc.psum_tensor(f"pT4{i}", [128, T], F32)) for i in range(2)])

        for tb in range(c.NT):
            tsl = slice(tb * T, (tb + 1) * T)
            for s_ in range(c.SUBT):
                for (s2, v2) in x2_rel[s_]:
                    mk.wait("sy", s2, v2)
                x2_rel[s_] = []
                row = T + tb * T + s_ * 128
                mk.dma("sy", x2b[s_][:], x_in[row:row + 128, :], x2_ld[s_])
            mk.wait("ve", mk.tc)
            for ec in range(c.EC):
                yts = ytr.load(sc_yt[ec * 128:(ec + 1) * 128, tsl])
                gts = gtr.load(sc_gate[ec * 128:(ec + 1) * 128, tsl])
                xcs = xcr.load(sc_xconv[ec * 128:(ec + 1) * 128, tsl])
                gslot = sgb.acquire("se")
                gtr.cwait("se", gts)
                mk.act(gslot.t[:], gts.t[:], AF.Silu, sem=mk.sc)
                gtr.record(gts, mk.sc)
                xcr.cwait("ve", xcs)
                ytr.cwait("ve", yts)
                mk.stt(ydb[:], xcs.t[:], Dl_t[:, ec:ec + 1], yts.t[:],
                       ALU.mult, ALU.add)
                mk.wait("ve", mk.sc)
                mk.tt(ygb[:, ec * T:(ec + 1) * T], ydb[:], gslot.t[:],
                      ALU.mult, sem=mk.vc)
                xcr.record(xcs, mk.vc)
                ytr.record(yts, mk.vc)
                sgb.record(gslot, mk.vc)
            mk.wait("te", mk.vc)
            for m in range(c.HC):
                if m < nA:
                    def lchunk(k, m=m):
                        return woA[:, (k * nA + m) * 128:(k * nA + m + 1) * 128]
                else:
                    wslot = wob.load(wo_src[:, :, m * 128:(m + 1) * 128])
                    wob.cwait("te", wslot)
                    def lchunk(k, w=wslot):
                        return w.t[:, k * 128:(k + 1) * 128]
                po = pso.acquire("te")
                for k in range(c.EC):
                    mk.mm(po.t[:], lchunk(k), ygb[:, k * T:(k + 1) * T],
                          start=(k == 0), stop=(k == c.EC - 1),
                          sem=(mk.tc if k == c.EC - 1 else None))
                if m >= nA:
                    wob.record(wslot, mk.tc)
                mk.wait("se", mk.tc)
                oslot = ostg.acquire("se")
                mk.act(oslot.t[:], po.t[:], AF.Copy, sem=mk.sc)
                pso.record(po, mk.sc)
                pt = psT.acquire("te")
                mk.wait("te", mk.sc)
                for s_ in range(c.SUBT):
                    mk.trans(pt.t[:, s_ * 128:(s_ + 1) * 128],
                             oslot.t[:, s_ * 128:(s_ + 1) * 128], ident_f[:],
                             sem=(mk.tc if s_ == c.SUBT - 1 else None))
                ostg.record(oslot, mk.tc)
                mk.wait("ve", mk.tc)
                for s_ in range(c.SUBT):
                    mk.wait("ve", x2_ld[s_], 16 * (tb + 1))
                    mk.tt(x2b[s_][:, m * 128:(m + 1) * 128],
                          x2b[s_][:, m * 128:(m + 1) * 128],
                          pt.t[:, s_ * 128:(s_ + 1) * 128], ALU.add,
                          sem=(mk.vc if s_ == c.SUBT - 1 else None))
                psT.record(pt, mk.vc)
            mk.wait("sy", mk.vc)
            for s_ in range(c.SUBT):
                row = tb * T + s_ * 128
                mk.dma("sy", sc_x2[row:row + 128, :], x2b[s_][:], x2_ss[s_])
                x2_rel[s_].append((x2_ss[s_], x2_ss[s_].count))
        mk.barrier()

    with contextlib.ExitStack() as p5:
        def csb(name, shape, dt):
            return p5.enter_context(nc.sbuf_tensor(name, list(shape), dt))
        xr5 = mk.ring("xr5", [csb(f"xr5{i}", (128, c.H), F32)
                              for i in range(3)], "sy")
        dump = csb("dump5", (128, c.H), F32)
        stat = csb("stat5", (128, 8), F32)
        xnb = mk.rot([csb(f"xnb5{i}", (128, c.H), BF16) for i in range(2)])
        jb = [csb(f"jb{j}", (128, T), BF16) for j in range(c.HC)]
        jb_ss = mk.sem("jbsall")
        tp = mk.rot([p5.enter_context(
            nc.psum_tensor(f"tp5{i}", [128, 512], BF16)) for i in range(4)])

        for t in range(c.NT):
            for s_ in range(c.SUBT):
                row = t * T + s_ * 128
                xs = xr5.load(sc_x2[row:row + 128, :])
                xs_r = xs
                xr5.cwait("ve", xs)
                m_ = stat[:, 0:1]
                v_ = stat[:, 1:2]
                r_ = stat[:, 2:3]
                q_ = stat[:, 3:4]
                mk.reduce(m_, xs.t[:], ALU.add)
                mk.ts(m_, m_, 1.0 / c.H, ALU.mult, sem=mk.vc)
                xr5.cwait("se", xs)
                mk.act(dump[:], xs.t[:], AF.Square, accum=q_, sem=mk.sc)
                mk.wait("ve", mk.sc)
                mk.ts(v_, q_, 1.0 / c.H, ALU.mult)
                mk.ts(r_, m_, m_, ALU.mult)
                mk.tt(v_, v_, r_, ALU.subtract)
                mk.ts(v_, v_, 0.0, ALU.max, sem=mk.vc)
                mk.wait("se", mk.vc)
                mk.act(r_, v_, AF.Ln, bias=1e-5)
                mk.act(r_, r_, AF.Exp, scale=-0.5, sem=mk.sc)
                xslot = xnb.acquire("ve")
                mk.wait("ve", mk.sc)
                mk.ts(xslot.t[:], xs.t[:], m_, ALU.subtract, s2=r_,
                      op2=ALU.mult, sem=mk.vc)
                xr5.record(xs_r, mk.vc)
                for g0 in range(0, c.HC, 4):
                    gn = min(4, c.HC - g0)
                    pslot = tp.acquire("te")
                    mk.wait("te", mk.vc)
                    for jj in range(gn):
                        j = g0 + jj
                        mk.trans(pslot.t[:, jj * 128:(jj + 1) * 128],
                                 xslot.t[:, j * 128:(j + 1) * 128],
                                 ident_b[:],
                                 sem=(mk.tc if jj == gn - 1 else None))
                    mk.wait("se", mk.tc)
                    if s_ == 0:
                        mk.wait("se", jb_ss, jb_ss.count)
                    for jj in range(gn):
                        j = g0 + jj
                        mk.act(jb[j][:, s_ * 128:(s_ + 1) * 128],
                               pslot.t[:, jj * 128:(jj + 1) * 128], AF.Copy,
                               sem=(mk.sc if jj == gn - 1 else None))
                    tp.record(pslot, mk.sc)
                xnb.record(xslot, mk.tc)
            mk.wait("sy", mk.sc)
            for j in range(c.HC):
                mk.dma("sy", sc_x2nT[j * 128:(j + 1) * 128, t * T:(t + 1) * T],
                       jb[j][:], jb_ss)
        mk.barrier()

    with contextlib.ExitStack() as p6:
        def csb(name, shape, dt):
            return p6.enter_context(nc.sbuf_tensor(name, list(shape), dt))
        wa_t = csb("wa_t", (128, c.HC * c.NE), BF16)
        rb2_t = csb("rb2_t", (c.NE, c.H), BF16)
        sb2l_t = csb("sb2l_t", (128, c.NSH * c.HC), F32)
        sb2s_t = csb("sb2s_t", (128, c.HC), F32)
        rgb_t = csb("rgb_t", (128, c.NE * c.FC), F32)
        rb1_t = csb("rb1_t", (128, c.NE * c.FC), F32)
        sgb_t = csb("sgb_t", (128, c.NSH * c.FC), F32)
        sb1_t = csb("sb1_t", (128, c.NSH * c.FC), F32)
        sel16_t = csb("sel16_t", (c.NE, c.NE * 128), BF16)
        mk.dma("gp", wa_t[:], wa_in.rearrange("(k p) e -> p k e", p=128),
               s_gld)
        mk.dma("gp", rb2_t[:], rb2_in, s_gld)
        mk.dma("sy", sb2l_t[:], sb2_in, s_ld)
        mk.dma("sy", rgb_t[:], rgb_in, s_ld)
        mk.dma("sy", rb1_t[:], rb1_in, s_ld)
        mk.dma("sy", sgb_t[:], sgb_in, s_ld)
        mk.dma("sy", sb1_t[:], sb1_in, s_ld)
        mk.dma("gp", sel16_t[:], sel16_in, s_gld)
        mk.wait("ve", s_ld)
        mk.wait("te", s_gld)
        mk.wait("te", s_ld)
        mk.wait("se", s_ld)
        mk.tt(sb2s_t[:], sb2l_t[:, 0:c.HC], sb2l_t[:, c.HC:2 * c.HC],
              ALU.add, sem=mk.vc)

        x2h = [csb(f"x2h{k}", (128, c.THALF), BF16) for k in range(c.HC)]
        x2h_ss = mk.sem("x2hall")
        x2h_rel = []
        accb = csb("accb", (128, c.HC * c.THALF), BF16)
        hband = csb("hband", (128, c.FC * c.THALF), BF16)
        comb_b = csb("comb_b", (c.NE, c.THALF), BF16)
        cbb = csb("cbb", (128, c.THALF), BF16)
        affp = csb("affp", (128, c.THALF), F32)
        ctp = csb("ctp", (128, 128), F32)
        mk.memset(affp[:], 0.0, sem=mk.vc)
        mk.memset(ctp[:], 0.0, sem=mk.vc)
        mk.wait("se", mk.vc)
        mk.wait("te", mk.vc)
        rt1 = csb("rt1", (128, c.NE), F32)
        rt2 = csb("rt2", (128, c.NE), F32)
        rcol = csb("rcol", (128, 4), F32)
        hg = mk.rot([csb(f"hg{i}", (128, T), BF16) for i in range(2)])
        hw = mk.rot([csb(f"hw{i}", (128, T), BF16) for i in range(2)])
        htmp = csb("htmp", (128, T), BF16)
        outb = mk.rot([csb(f"outb{i}", (128, c.H), F32) for i in range(2)])
        out_ss = [mk.sem(f"oss{i}") for i in range(2)]
        a2f = mk.rot([csb(f"a2f{i}", (128, 128), F32) for i in range(2)])
        out_ld = [mk.sem(f"old{i}") for i in range(2)]
        out_ld_i = [0, 0]
        gring = mk.ring("gr", [csb(f"gr{i}", (128, c.HC * 128), BF16)
                               for i in range(3)], "gp")
        w1ring = mk.ring("w1r", [csb(f"w1r{i}", (128, c.HC * 128), BF16)
                                 for i in range(3)], "gp")
        w2ring = mk.ring("w2r", [csb(f"w2r{i}", (128, c.FC * 128), BF16)
                                 for i in range(4)], "gp")
        pg6 = mk.rot([p6.enter_context(
            nc.psum_tensor(f"pg6{i}", [128, T], F32)) for i in range(2)])
        pw6 = mk.rot([p6.enter_context(
            nc.psum_tensor(f"pw6{i}", [128, T], F32)) for i in range(2)])
        po6 = mk.rot([p6.enter_context(
            nc.psum_tensor(f"po6{i}", [128, T], F32)) for i in range(2)])
        pr6 = mk.rot([p6.enter_context(
            nc.psum_tensor(f"pr6{i}", [128, 512], F32)) for i in range(2)])

        for hf in range(c.NFQ):
            ht = slice(hf * c.THALF, (hf + 1) * c.THALF)
            for (s2, v2) in x2h_rel:
                mk.wait("sy", s2, v2)
            x2h_rel = []
            for k in range(c.HC):
                mk.dma("sy", x2h[k][:], sc_x2nT[k * 128:(k + 1) * 128, ht],
                       x2h_ss)
            mk.wait("te", x2h_ss, x2h_ss.count)
            for tt in range(c.NTH):
                pa = pr6.acquire("te")
                for k in range(c.HC):
                    mk.mm(pa.t[0:c.NE, 0:T], wa_t[:, k * c.NE:(k + 1) * c.NE],
                          x2h[k][:, tt * T:(tt + 1) * T],
                          start=(k == 0), stop=(k == c.HC - 1),
                          sem=(mk.tc if k == c.HC - 1 else None))
                mk.wait("se", mk.tc)
                mk.act(affp[0:c.NE, tt * T:(tt + 1) * T], pa.t[0:c.NE, 0:T],
                       AF.Tanh, scale=0.5, sem=mk.sc)
                pr6.record(pa, mk.sc)
            mk.wait("ve", mk.sc)
            mk.ts(affp[0:c.NE, :], affp[0:c.NE, :], 0.5, ALU.mult, s2=0.5,
                  op2=ALU.add, sem=mk.vc)
            for sub in range(c.NSUBH):
                pa = pr6.acquire("te")
                mk.wait("te", mk.vc)
                mk.trans(pa.t[:, 0:128], affp[:, sub * 128:(sub + 1) * 128],
                         ident_f[:], sem=mk.tc)
                mk.wait("ve", mk.tc)
                aff16 = pa.t[:, 0:c.NE]
                mk.reduce(rcol[:, 0:1], aff16, ALU.max)
                mk.ts(rt1[:], aff16, rcol[:, 0:1], ALU.is_equal)
                mk.stt(rt2[:], rt1[:], -1e30, aff16, ALU.mult, ALU.add)
                mk.reduce(rcol[:, 1:2], rt2[:], ALU.max)
                mk.ts(rt2[:], rt2[:], rcol[:, 1:2], ALU.is_equal)
                mk.tt(rt1[:], rt1[:], rt2[:], ALU.add)
                mk.tt(rcol[:, 2:3], rcol[:, 0:1], rcol[:, 1:2], ALU.add)
                mk.tt(rt1[:], rt1[:], aff16, ALU.mult, sem=mk.vc)
                pr6.record(pa, mk.vc)
                mk.wait("ve", mk.vc)
                mk.vrecip(rcol[:, 3:4], rcol[:, 2:3])
                mk.ts(ctp[:, 0:c.NE], rt1[:], rcol[:, 3:4], ALU.mult,
                      sem=mk.vc)
                pb = pr6.acquire("te")
                mk.wait("te", mk.vc)
                mk.trans(pb.t[:, 0:128], ctp[:], ident_f[:], sem=mk.tc)
                mk.wait("ve", mk.tc)
                mk.copy(comb_b[0:c.NE, sub * 128:(sub + 1) * 128],
                        pb.t[0:c.NE, 0:128], sem=mk.vc)
                pr6.record(pb, mk.vc)
            if dbg:
                mk.wait("sy", mk.vc)
                mk.dma("sy", dbg_comb[0:c.NE, hf * c.THALF:(hf + 1) * c.THALF],
                       comb_b[0:c.NE, :], s_st)
            for e in range(c.NE + c.NSH):
                routed = e < c.NE
                if routed:
                    g_src, w1_src, w2_src = rg_in, rw1_in, rw2_in
                    gb_col = rgb_t[:, e * c.FC:(e + 1) * c.FC]
                    b1_col = rb1_t[:, e * c.FC:(e + 1) * c.FC]
                    ei = e
                else:
                    g_src, w1_src, w2_src = sg_in, sw1_in, sw2_in
                    gb_col = sgb_t[:, (e - c.NE) * c.FC:
                                   (e - c.NE + 1) * c.FC]
                    b1_col = sb1_t[:, (e - c.NE) * c.FC:
                                   (e - c.NE + 1) * c.FC]
                    ei = e - c.NE
                if routed:
                    for tt in range(c.NTH):
                        pa = pr6.acquire("te")
                        mk.wait("te", mk.vc)
                        mk.mm(pa.t[:], sel16_t[:, e * 128:(e + 1) * 128],
                              comb_b[0:c.NE, tt * T:(tt + 1) * T],
                              start=True, stop=True, sem=mk.tc)
                        mk.wait("se", mk.vc)
                        mk.wait("se", mk.tc)
                        mk.act(cbb[:, tt * T:(tt + 1) * T], pa.t[:], AF.Copy,
                               sem=mk.sc)
                        pr6.record(pa, mk.sc)
                    mk.wait("ve", mk.sc)
                mk.wait("ve", mk.tc)
                for fm in range(c.FC):
                    gsl = gring.load(
                        g_src[ei, :, fm * 128:(fm + 1) * 128]
                        .rearrange("(k p) f -> p k f", p=128))
                    w1sl = w1ring.load(
                        w1_src[ei, :, fm * 128:(fm + 1) * 128]
                        .rearrange("(k p) f -> p k f", p=128))
                    gring.cwait("te", gsl)
                    w1ring.cwait("te", w1sl)
                    for tt in range(c.NTH):
                        pg = pg6.acquire("te")
                        pw = pw6.acquire("te")
                        for k in range(c.HC):
                            mk.mm(pg.t[:], gsl.t[:, k * 128:(k + 1) * 128],
                                  x2h[k][:, tt * T:(tt + 1) * T],
                                  start=(k == 0), stop=(k == c.HC - 1),
                                  sem=(mk.tc if k == c.HC - 1 else None))
                        for k in range(c.HC):
                            mk.mm(pw.t[:], w1sl.t[:, k * 128:(k + 1) * 128],
                                  x2h[k][:, tt * T:(tt + 1) * T],
                                  start=(k == 0), stop=(k == c.HC - 1),
                                  sem=(mk.tc if k == c.HC - 1 else None))
                        if tt == c.NTH - 1:
                            gring.record(gsl, mk.tc)
                            w1ring.record(w1sl, mk.tc)
                        mk.wait("se", mk.tc)
                        g1 = hg.acquire("se")
                        mk.act(g1.t[:], pg.t[:], AF.Gelu,
                               bias=gb_col[:, fm:fm + 1], sem=mk.sc)
                        pg6.record(pg, mk.sc)
                        w1_ = hw.acquire("se")
                        mk.act(w1_.t[:], pw.t[:], AF.Identity,
                               bias=b1_col[:, fm:fm + 1], sem=mk.sc)
                        pw6.record(pw, mk.sc)
                        mk.wait("ve", mk.sc)
                        hcol = hband[:, fm * c.THALF + tt * T:
                                     fm * c.THALF + (tt + 1) * T]
                        if routed:
                            mk.tt(htmp[:], g1.t[:], w1_.t[:], ALU.mult)
                            mk.tt(hcol, htmp[:], cbb[:, tt * T:(tt + 1) * T],
                                  ALU.mult, sem=mk.vc)
                        else:
                            mk.tt(hcol, g1.t[:], w1_.t[:], ALU.mult,
                                  sem=mk.vc)
                        hg.record(g1, mk.vc)
                        hw.record(w1_, mk.vc)
                mk.wait("te", mk.vc)
                for dm in range(c.HC):
                    w2sl = w2ring.load(
                        w2_src[ei, :, dm * 128:(dm + 1) * 128]
                        .rearrange("(k p) d -> p k d", p=128))
                    w2ring.cwait("te", w2sl)
                    for tt in range(c.NTH):
                        po = po6.acquire("te")
                        for fk in range(c.FC):
                            mk.mm(po.t[:], w2sl.t[:, fk * 128:(fk + 1) * 128],
                                  hband[:, fk * c.THALF + tt * T:
                                        fk * c.THALF + (tt + 1) * T],
                                  start=(fk == 0), stop=(fk == c.FC - 1),
                                  sem=(mk.tc if fk == c.FC - 1 else None))
                        if tt == c.NTH - 1:
                            w2ring.record(w2sl, mk.tc)
                        mk.wait("ve", mk.tc)
                        asl = accb[:, dm * c.THALF + tt * T:
                                   dm * c.THALF + (tt + 1) * T]
                        if e == 0:
                            mk.tt(asl, x2h[dm][:, tt * T:(tt + 1) * T],
                                  po.t[:], ALU.add, sem=mk.vc)
                        else:
                            mk.tt(asl, asl, po.t[:], ALU.add, sem=mk.vc)
                        po6.record(po, mk.vc)
            x2h_rel.append((mk.tc, mk.tc.count))
            for dm in range(c.HC):
                for tt in range(c.NTH):
                    po = po6.acquire("te")
                    mk.mm(po.t[:], rb2_t[:, dm * 128:(dm + 1) * 128],
                          comb_b[0:c.NE, tt * T:(tt + 1) * T],
                          start=True, stop=True, sem=mk.tc)
                    mk.wait("ve", mk.tc)
                    asl = accb[:, dm * c.THALF + tt * T:
                               dm * c.THALF + (tt + 1) * T]
                    mk.tt(asl, asl, po.t[:], ALU.add)
                    mk.ts(asl, asl, sb2s_t[:, dm:dm + 1], ALU.add,
                          sem=mk.vc)
                    po6.record(po, mk.vc)
            for sub in range(c.NSUBH):
                osl = outb.acquire("sy")
                row = hf * c.THALF + sub * 128
                oi = osl.i
                mk.dma("sy", osl.t[:], sc_x2[row:row + 128, :], out_ld[oi])
                for dm in range(c.HC):
                    af = a2f.acquire("ve")
                    mk.copy(af.t[:],
                            accb[:, dm * c.THALF + sub * 128:
                                 dm * c.THALF + (sub + 1) * 128], sem=mk.vc)
                    pt = pr6.acquire("te")
                    mk.wait("te", mk.vc)
                    mk.trans(pt.t[:, 0:128], af.t[:], ident_f[:], sem=mk.tc)
                    a2f.record(af, mk.tc)
                    mk.wait("ve", mk.tc)
                    mk.wait("ve", out_ld[oi], out_ld[oi].count)
                    mk.tt(osl.t[:, dm * 128:(dm + 1) * 128],
                          osl.t[:, dm * 128:(dm + 1) * 128],
                          pt.t[:, 0:128], ALU.add,
                          sem=(mk.vc if dm == c.HC - 1 else None))
                    pr6.record(pt, mk.vc)
                mk.wait("sy", mk.vc)
                mk.dma("sy", out_d[row:row + 128, :], osl.t[:], out_ss[oi])
                outb.record(osl, out_ss[oi])
        for ss in out_ss:
            mk.wait("sy", ss, ss.count)
    mk.barrier()

    mk.finalize_and_emit()
    mk.stack.close()
    mybir.codegen_inst_isa_subclasses(nc)
    return nc



def _lay(a):
    return np.ascontiguousarray(a)


def prep_inputs(c, ins):
    x = np.asarray(ins["x"], np.float32)
    conv_w = np.asarray(ins["conv_w"], np.float32)
    common = {
        "w_in_T": _lay(np.asarray(ins["in_proj_w"], np.float32).T),
        "w_xp_T": _lay(np.asarray(ins["x_proj_w"], np.float32).T),
        "w_dt_T": _lay(np.asarray(ins["dt_proj_w"], np.float32).T),
        "w_out_T": _lay(np.asarray(ins["out_proj_w"], np.float32).T),
        "conv_w_l": _lay(conv_w[:, 0, :].reshape(c.EC, 128, c.KCONV)
                         .transpose(1, 0, 2).reshape(128, c.EC * c.KCONV)),
        "conv_b_l": _lay(np.asarray(ins["conv_b"], np.float32)
                         .reshape(c.EC, 128).T),
        "dt_b_l": _lay(np.asarray(ins["dt_proj_b"], np.float32)
                       .reshape(c.EC, 128).T),
        "D_l": _lay(np.asarray(ins["D"], np.float32).reshape(c.EC, 128).T),
        "A_log_l": _lay(np.asarray(ins["A_log"], np.float32)
                        .reshape(c.EC, 128, c.N).transpose(1, 0, 2)
                        .reshape(128, c.ECN)),
        "ident": np.eye(128, dtype=np.float32),
        "sel32": _lay(np.repeat(np.eye(c.M2, dtype=np.float32), 128, axis=1)),
        "sel16": _lay(np.repeat(np.eye(c.NE, dtype=np.float32), 128, axis=1)),
        "wa": np.asarray(ins["wa"], np.float32),
        "rg": np.asarray(ins["rg"], np.float32),
        "rw1": np.asarray(ins["rw1"], np.float32),
        "rw2": np.asarray(ins["rw2"], np.float32),
        "rgb_l": _lay(np.asarray(ins["rgb"], np.float32)
                      .reshape(c.NE, c.FC, 128).transpose(2, 0, 1)
                      .reshape(128, c.NE * c.FC)),
        "rb1_l": _lay(np.asarray(ins["rb1"], np.float32)
                      .reshape(c.NE, c.FC, 128).transpose(2, 0, 1)
                      .reshape(128, c.NE * c.FC)),
        "rb2": np.asarray(ins["rb2"], np.float32),
        "sg": np.asarray(ins["sg"], np.float32),
        "sw1": np.asarray(ins["sw1"], np.float32),
        "sw2": np.asarray(ins["sw2"], np.float32),
        "sgb_l": _lay(np.asarray(ins["sgb"], np.float32)
                      .reshape(c.NSH, c.FC, 128).transpose(2, 0, 1)
                      .reshape(128, c.NSH * c.FC)),
        "sb1_l": _lay(np.asarray(ins["sb1"], np.float32)
                      .reshape(c.NSH, c.FC, 128).transpose(2, 0, 1)
                      .reshape(128, c.NSH * c.FC)),
        "sb2_l": _lay(np.asarray(ins["sb2"], np.float32)
                      .reshape(c.NSH, c.HC, 128).transpose(2, 0, 1)
                      .reshape(128, c.NSH * c.HC)),
    }
    maps = []
    for core in range(c.NCORES):
        b = core // c.SEQC
        ch = core % c.SEQC
        s0 = ch * c.TCORE
        xt = np.zeros((c.TIN, c.H), np.float32)
        lo = s0 - c.TILE
        src_lo = max(lo, 0)
        xt[src_lo - lo:, :] = x[b, src_lo:s0 + c.TCORE, :]
        mask = np.zeros(c.NCORES, np.float32)
        for j in range(b * c.SEQC, core):
            mask[j] = 1.0
        m = dict(common)
        m["x"] = xt
        m["mask8_l"] = _lay(np.broadcast_to(mask, (128, c.NCORES)).copy())
        maps.append(m)
    return maps


_CACHE = {}


def _full_input_dict(c, kw, prefix_attn="L0_attn_", prefix_ffn="L0_ffn_"):
    return {
        "x": kw[prefix_attn + "hidden_states"],
        "in_proj_w": kw[prefix_attn + "in_proj_w"],
        "conv_w": kw[prefix_attn + "conv_w"],
        "conv_b": kw[prefix_attn + "conv_b"],
        "x_proj_w": kw[prefix_attn + "x_proj_w"],
        "dt_proj_w": kw[prefix_attn + "dt_proj_w"],
        "dt_proj_b": kw[prefix_attn + "dt_proj_b"],
        "A_log": kw[prefix_attn + "A_log"],
        "D": kw[prefix_attn + "D"],
        "out_proj_w": kw[prefix_attn + "out_proj_w"],
        "wa": kw[prefix_ffn + "wa"],
        "rg": kw[prefix_ffn + "rg"],
        "rgb": kw[prefix_ffn + "rgb"],
        "rw1": kw[prefix_ffn + "rw1"],
        "rb1": kw[prefix_ffn + "rb1"],
        "rw2": kw[prefix_ffn + "rw2"],
        "rb2": kw[prefix_ffn + "rb2"],
        "sg": kw[prefix_ffn + "sg"],
        "sgb": kw[prefix_ffn + "sgb"],
        "sw1": kw[prefix_ffn + "sw1"],
        "sb1": kw[prefix_ffn + "sb1"],
        "sw2": kw[prefix_ffn + "sw2"],
        "sb2": kw[prefix_ffn + "sb2"],
    }


def run_cores(c, ins, dbg=False, raw=False):
    key = (c.H, c.E, c.FF, c.TCORE, c.NE, c.NSH, dbg)
    if "nc" not in _CACHE or _CACHE.get("key") != key:
        _CACHE["nc"] = build_program(c, dbg=dbg)
        _CACHE["key"] = key
    maps = prep_inputs(c, ins)
    res = run_bass_kernel_spmd(_CACHE["nc"], maps,
                               core_ids=list(range(c.NCORES)))
    if raw:
        return res
    out = np.empty((c.B, c.SEQC * c.TCORE, c.H), np.float32)
    for core in range(c.NCORES):
        b = core // c.SEQC
        ch = core % c.SEQC
        out[b, ch * c.TCORE:(ch + 1) * c.TCORE, :] = res.results[core]["out"]
    return out


def kernel(**kw):
    c = make_cfg(full=True)
    ins = _full_input_dict(c, kw)
    return run_cores(c, ins)
